# version 02 (final)
# baseline (speedup 1.0000x reference)
import contextlib
import math
import numpy as np
import ml_dtypes
import concourse.bass as bass
import concourse.mybir as mybir
from concourse.bass_utils import run_bass_kernel_spmd

F32 = mybir.dt.float32
BF16 = mybir.dt.bfloat16
AF = mybir.ActivationFunctionType
ALU = mybir.AluOpType
AX = mybir.AxisListType.X
BF = ml_dtypes.bfloat16

B, S, D, H, HD, LD, FF, E = 4, 4096, 1024, 16, 64, 256, 4096, 16
CAP = 256
NEG = -1.0e30


class Prog:
    def __init__(self):
        self.nc = bass.Bass("TRN2", target_bir_lowering=False)
        self.steps = []

    def step(self, eng, fn, dma=False):
        self.steps.append((eng, fn, 16 if dma else 1))

    def dma(self, out, in_):
        nc = self.nc
        self.step("sync", lambda o=out, i=in_: nc.sync.dma_start(out=o, in_=i), dma=True)

    def build(self, stack):
        nc = self.nc
        tk = stack.enter_context(nc.semaphore())
        blk = stack.enter_context(nc.Block())
        vals, c = [], 0
        for (_, _, inc) in self.steps:
            vals.append(c)
            c += inc
        total = c
        used = sorted(set(e for e, _, _ in self.steps) | {"scalar"})
        for en in used:
            def mk(engname):
                def body(eng):
                    for (e2, fn, inc), v in zip(self.steps, vals):
                        if e2 != engname:
                            continue
                        eng.wait_ge(tk, v)
                        fn().then_inc(tk, inc)
                    if engname == "scalar":
                        eng.wait_ge(tk, total)
                return body
            getattr(blk, en)(mk(en))
        return nc


def build_ln():
    p = Prog()
    nc = p.nc
    st = contextlib.ExitStack()
    x = nc.dram_tensor("x", [2048, D], F32, kind="ExternalInput").ap()
    y = nc.dram_tensor("y", [2048, D], BF16, kind="ExternalOutput").ap()
    xt = st.enter_context(nc.sbuf_tensor("xt", [128, D], F32))
    xc = st.enter_context(nc.sbuf_tensor("xc", [128, D], F32))
    sq = st.enter_context(nc.sbuf_tensor("sq", [128, D], F32))
    yt = st.enter_context(nc.sbuf_tensor("yt", [128, D], BF16))
    m = st.enter_context(nc.sbuf_tensor("m", [128, 1], F32))
    v = st.enter_context(nc.sbuf_tensor("v", [128, 1], F32))
    r = st.enter_context(nc.sbuf_tensor("r", [128, 1], F32))
    for t in range(16):
        p.dma(xt[:, :], x[t * 128:(t + 1) * 128, :])
        p.step("vector", lambda: nc.vector.reduce_sum(m[:, :], xt[:, :], axis=AX))
        p.step("scalar", lambda: nc.scalar.mul(m[:, :], m[:, :], 1.0 / D))
        p.step("vector", lambda: nc.vector.tensor_scalar(xc[:, :], xt[:, :], m[:, 0:1], None, ALU.subtract))
        p.step("vector", lambda: nc.vector.tensor_mul(sq[:, :], xc[:, :], xc[:, :]))
        p.step("vector", lambda: nc.vector.reduce_sum(v[:, :], sq[:, :], axis=AX))
        p.step("scalar", lambda: nc.scalar.mul(v[:, :], v[:, :], 1.0 / D))
        p.step("vector", lambda: nc.vector.tensor_scalar_add(v[:, :], v[:, :], 1e-5))
        p.step("scalar", lambda: nc.scalar.activation(v[:, :], v[:, :], AF.Sqrt))
        p.step("vector", lambda: nc.vector.reciprocal(r[:, :], v[:, :]))
        p.step("vector", lambda: nc.vector.tensor_scalar(yt[:, :], xc[:, :], r[:, 0:1], None, ALU.mult))
        p.dma(y[t * 128:(t + 1) * 128, :], yt[:, :])
    nc = p.build(st)
    st.close()
    return nc


def build_attn0():
    p = Prog()
    nc = p.nc
    st = contextlib.ExitStack()
    xnT = nc.dram_tensor("xnT", [D, S], BF16, kind="ExternalInput").ap()
    xnO = nc.dram_tensor("xnO", [D, 2048], BF16, kind="ExternalInput").ap()
    wq_d = nc.dram_tensor("wq", [D, 512], F32, kind="ExternalInput").ap()
    wkv_d = nc.dram_tensor("wkv", [D, LD], F32, kind="ExternalInput").ap()
    wku_d = nc.dram_tensor("wku", [LD, 512], F32, kind="ExternalInput").ap()
    wvu_d = nc.dram_tensor("wvu", [LD, 512], F32, kind="ExternalInput").ap()
    wo_d = nc.dram_tensor("wo", [512, D], F32, kind="ExternalInput").ap()
    qidx_d = nc.dram_tensor("qidx", [128, 16], F32, kind="ExternalInput").ap()
    iota_d = nc.dram_tensor("iota", [128, S], F32, kind="ExternalInput").ap()
    idb_d = nc.dram_tensor("identb", [128, 128], BF16, kind="ExternalInput").ap()
    s_out = nc.dram_tensor("s", [2048, D], F32, kind="ExternalOutput").ap()

    stage = st.enter_context(nc.sbuf_tensor("stage", [128, 1024], F32))
    wq = [st.enter_context(nc.sbuf_tensor(f"wq{i}", [128, 512], BF16)) for i in range(8)]
    wkv = [st.enter_context(nc.sbuf_tensor(f"wkv{i}", [128, LD], BF16)) for i in range(8)]
    wku = [st.enter_context(nc.sbuf_tensor(f"wku{i}", [128, 512], BF16)) for i in range(2)]
    wvu = [st.enter_context(nc.sbuf_tensor(f"wvu{i}", [128, 512], BF16)) for i in range(2)]
    wo = [st.enter_context(nc.sbuf_tensor(f"wo{i}", [64, 1024], BF16)) for i in range(8)]
    lat = [st.enter_context(nc.sbuf_tensor(f"lat{i}", [128, S], BF16)) for i in range(2)]
    kt = [st.enter_context(nc.sbuf_tensor(f"kt{i}", [128, S], BF16)) for i in range(4)]
    vsb = st.enter_context(nc.sbuf_tensor("vsb", [128, 32 * 512], BF16))
    xs = [st.enter_context(nc.sbuf_tensor(f"xs{i}", [128, 512], BF16)) for i in range(8)]
    qt = [st.enter_context(nc.sbuf_tensor(f"qt{i}", [128, 128], BF16)) for i in range(4)]
    sc = st.enter_context(nc.sbuf_tensor("sc", [128, S], F32))
    msk = st.enter_context(nc.sbuf_tensor("msk", [128, S], F32))
    pb = st.enter_context(nc.sbuf_tensor("pb", [128, S], BF16))
    iot = st.enter_context(nc.sbuf_tensor("iot", [128, S], F32))
    qix = st.enter_context(nc.sbuf_tensor("qix", [128, 16], F32))
    idb = st.enter_context(nc.sbuf_tensor("idb", [128, 128], BF16))
    pt = st.enter_context(nc.sbuf_tensor("pt", [128, 128], BF16))
    yt = st.enter_context(nc.sbuf_tensor("yt", [64, 128], BF16))
    osb = st.enter_context(nc.sbuf_tensor("osb", [128, 1024], F32))
    mx = st.enter_context(nc.sbuf_tensor("mx", [128, 1], F32))
    nm = st.enter_context(nc.sbuf_tensor("nm", [128, 1], F32))
    sm = st.enter_context(nc.sbuf_tensor("sm", [128, 1], F32))
    rc = st.enter_context(nc.sbuf_tensor("rc", [128, 1], F32))

    ps_sc = st.enter_context(nc.psum_tensor("ps_sc", [128, 512], F32))
    ps_tp = st.enter_context(nc.psum_tensor("ps_tp", [128, 128], BF16))
    ps_pv = st.enter_context(nc.psum_tensor("ps_pv", [64, 128], F32))
    ps_o = st.enter_context(nc.psum_tensor("ps_o", [128, 1024], F32))
    ps_p = st.enter_context(nc.psum_tensor("ps_p", [128, 512], F32))

    def ldcast(dst_bf, dram_ap, w):
        p.dma(stage[:, 0:w], dram_ap)
        p.step("vector", lambda d=dst_bf, w=w: nc.vector.tensor_copy(d[:, 0:w], stage[:, 0:w]))

    for i in range(8):
        ldcast(wq[i], wq_d[i * 128:(i + 1) * 128, :], 512)
    for i in range(8):
        ldcast(wkv[i], wkv_d[i * 128:(i + 1) * 128, :], LD)
    for i in range(2):
        ldcast(wku[i], wku_d[i * 128:(i + 1) * 128, :], 512)
    for i in range(2):
        ldcast(wvu[i], wvu_d[i * 128:(i + 1) * 128, :], 512)
    for i in range(8):
        p.dma(stage[0:64, 0:1024], wo_d[i * 64:(i + 1) * 64, :])
        p.step("vector", lambda i=i: nc.vector.tensor_copy(wo[i][:, :], stage[0:64, 0:1024]))
    p.dma(iot[:, :], iota_d[:, :])
    p.dma(qix[:, :], qidx_d[:, :])
    p.dma(idb[:, :], idb_d[:, :])

    for nch in range(8):
        for dc in range(8):
            p.dma(xs[dc][:, :], xnT[dc * 128:(dc + 1) * 128, nch * 512:(nch + 1) * 512])
        for cc in range(2):
            def f(cc=cc, nch=nch):
                for dc in range(8):
                    ins = nc.tensor.matmul(ps_p[:, :], wkv[dc][:, cc * 128:(cc + 1) * 128],
                                           xs[dc][:, :], start=(dc == 0), stop=(dc == 7))
                return ins
            p.step("tensor", f)
            p.step("scalar", lambda cc=cc, nch=nch: nc.scalar.copy(
                lat[cc][:, nch * 512:(nch + 1) * 512], ps_p[:, :]))
        for m4 in range(4):
            def f(m4=m4, nch=nch):
                for cc in range(2):
                    ins = nc.tensor.matmul(ps_p[:, :], wku[cc][:, m4 * 128:(m4 + 1) * 128],
                                           lat[cc][:, nch * 512:(nch + 1) * 512],
                                           start=(cc == 0), stop=(cc == 1))
                return ins
            p.step("tensor", f)
            p.step("scalar", lambda m4=m4, nch=nch: nc.scalar.copy(
                kt[m4][:, nch * 512:(nch + 1) * 512], ps_p[:, :]))
        for nt in range(4):
            ng = nch * 4 + nt
            def f(nt=nt, nch=nch):
                for cc in range(2):
                    ins = nc.tensor.matmul(ps_p[:, :],
                                           lat[cc][:, nch * 512 + nt * 128: nch * 512 + (nt + 1) * 128],
                                           wvu[cc][:, :], start=(cc == 0), stop=(cc == 1))
                return ins
            p.step("tensor", f)
            p.step("scalar", lambda ng=ng: nc.scalar.copy(
                vsb[:, ng * 512:(ng + 1) * 512], ps_p[:, :]))

    for j in range(16):
        W = (2 * j + 2) * 128
        for dc in range(8):
            p.dma(xs[dc][:, 0:128], xnO[dc * 128:(dc + 1) * 128, j * 128:(j + 1) * 128])
        for m4 in range(4):
            def f(m4=m4):
                for dc in range(8):
                    ins = nc.tensor.matmul(ps_p[:, 0:128], wq[dc][:, m4 * 128:(m4 + 1) * 128],
                                           xs[dc][:, 0:128], start=(dc == 0), stop=(dc == 7))
                return ins
            p.step("tensor", f)
            p.step("scalar", lambda m4=m4: nc.scalar.copy(qt[m4][:, :], ps_p[:, 0:128]))
        p.step("vector", lambda j=j, W=W: nc.vector.tensor_single_scalar(
            msk[:, 0:W], iot[:, 0:W], qix[:, j:j + 1], ALU.is_gt))
        for h in range(8):
            hh = h * 64
            qc, qo = hh // 128, hh % 128
            nchunks = W // 512 + (1 if W % 512 else 0)
            for c in range(nchunks):
                w = min(512, W - c * 512)
                def f(c=c, w=w, qc=qc, qo=qo):
                    return nc.tensor.matmul(ps_sc[:, 0:w], qt[qc][qo:qo + 64, :],
                                            kt[qc][qo:qo + 64, c * 512:c * 512 + w],
                                            start=True, stop=True, skip_group_check=True)
                p.step("tensor", f)
                p.step("scalar", lambda c=c, w=w: nc.scalar.activation(
                    sc[:, c * 512:c * 512 + w], ps_sc[:, 0:w], AF.Copy, scale=0.125))
            p.step("vector", lambda W=W: nc.vector.scalar_tensor_tensor(
                sc[:, 0:W], msk[:, 0:W], NEG, sc[:, 0:W], ALU.mult, ALU.add))
            p.step("vector", lambda W=W: nc.vector.reduce_max(mx[:, :], sc[:, 0:W], axis=AX))
            p.step("scalar", lambda: nc.scalar.mul(nm[:, :], mx[:, :], -1.0))
            p.step("scalar", lambda W=W: nc.scalar.activation(
                pb[:, 0:W], sc[:, 0:W], AF.Exp, bias=nm[:, 0:1]))
            p.step("vector", lambda W=W: nc.vector.reduce_sum(sm[:, :], pb[:, 0:W], axis=AX))
            p.step("vector", lambda: nc.vector.reciprocal(rc[:, :], sm[:, :]))
            p.step("vector", lambda W=W: nc.vector.tensor_scalar(
                pb[:, 0:W], pb[:, 0:W], rc[:, 0:1], None, ALU.mult))
            for kc in range(W // 128):
                p.step("tensor", lambda kc=kc: nc.tensor.transpose(
                    ps_tp[:, :], pb[:, kc * 128:(kc + 1) * 128], idb[:, :]))
                p.step("scalar", lambda: nc.scalar.copy(pt[:, :], ps_tp[:, :]))
                p.step("tensor", lambda kc=kc, hh=hh, last=(kc == W // 128 - 1): nc.tensor.matmul(
                    ps_pv[:, :], vsb[:, kc * 512 + hh:kc * 512 + hh + 64], pt[:, :],
                    start=(kc == 0), stop=last, skip_group_check=True))
            p.step("scalar", lambda: nc.scalar.copy(yt[:, :], ps_pv[:, :]))
            for nh in range(2):
                p.step("tensor", lambda nh=nh, h=h: nc.tensor.matmul(
                    ps_o[:, nh * 512:(nh + 1) * 512], yt[:, :],
                    wo[h][:, nh * 512:(nh + 1) * 512],
                    start=(h == 0), stop=(h == 7), skip_group_check=True))
        p.step("scalar", lambda: nc.scalar.copy(osb[:, :], ps_o[:, :]))
        p.dma(s_out[j * 128:(j + 1) * 128, :], osb[:, :])
    nc = p.build(st)
    st.close()
    return nc


def build_ffn0():
    p = Prog()
    nc = p.nc
    st = contextlib.ExitStack()
    xnO = nc.dram_tensor("xnO", [D, 2048], BF16, kind="ExternalInput").ap()
    w1t_d = nc.dram_tensor("w1t", [D, FF], F32, kind="ExternalInput").ap()
    w2t_d = nc.dram_tensor("w2t", [FF, D], F32, kind="ExternalInput").ap()
    b1_d = nc.dram_tensor("b1", [1, FF], F32, kind="ExternalInput").ap()
    b2_d = nc.dram_tensor("b2", [1, D], F32, kind="ExternalInput").ap()
    f_out = nc.dram_tensor("f", [2048, D], F32, kind="ExternalOutput").ap()

    stage = st.enter_context(nc.sbuf_tensor("stage", [128, FF], F32))
    tmpa = st.enter_context(nc.sbuf_tensor("tmpa", [128, FF], F32))
    w1s = [st.enter_context(nc.sbuf_tensor(f"w1s{i}", [128, FF], BF16)) for i in range(8)]
    w2s = [st.enter_context(nc.sbuf_tensor(f"w2s{i}", [128, D], BF16)) for i in range(32)]
    hs = [st.enter_context(nc.sbuf_tensor(f"hs{i}", [128, 128], BF16)) for i in range(32)]
    xs = [st.enter_context(nc.sbuf_tensor(f"xs{i}", [128, 128], BF16)) for i in range(8)]
    b1s = st.enter_context(nc.sbuf_tensor("b1s", [1, FF], F32))
    b2s = st.enter_context(nc.sbuf_tensor("b2s", [1, D], F32))
    ones = st.enter_context(nc.sbuf_tensor("ones", [1, 512], F32))
    onec = st.enter_context(nc.sbuf_tensor("onec", [128, 1], F32))
    acc = st.enter_context(nc.sbuf_tensor("acc", [128, 1], F32))
    tm1 = st.enter_context(nc.sbuf_tensor("tm1", [128, 1], F32))
    ssb = st.enter_context(nc.sbuf_tensor("ssb", [1, 1], F32))
    rs = st.enter_context(nc.sbuf_tensor("rs", [1, 1], F32))
    rs128 = st.enter_context(nc.sbuf_tensor("rs128", [128, 1], F32))
    osb = st.enter_context(nc.sbuf_tensor("osb", [128, D], F32))
    ps_1 = st.enter_context(nc.psum_tensor("ps_1", [1, 1], F32))
    ps_b = st.enter_context(nc.psum_tensor("ps_b", [128, 1], F32))
    ps_h = st.enter_context(nc.psum_tensor("ps_h", [128, 128], F32))
    ps_o = st.enter_context(nc.psum_tensor("ps_o", [128, 1024], F32))

    p.step("vector", lambda: nc.vector.memset(ones[:, :], 1.0))
    p.step("vector", lambda: nc.vector.memset(onec[:, :], 1.0))
    p.dma(b1s[:, :], b1_d[:, :])
    p.dma(b2s[:, :], b2_d[:, :])

    def ternarize(dram, nchunk, width, dst, which):
        p.step("vector", lambda: nc.vector.memset(acc[:, :], 0.0))
        for c in range(nchunk):
            p.dma(stage[:, 0:width], dram[c * 128:(c + 1) * 128, :])
            p.step("vector", lambda w=width: nc.vector.reduce_sum(
                tm1[:, :], stage[:, 0:w], axis=AX, apply_absolute_value=True))
            p.step("vector", lambda: nc.vector.tensor_add(acc[:, :], acc[:, :], tm1[:, :]))
        p.step("tensor", lambda: nc.tensor.matmul(ps_1[:, :], acc[:, :], onec[:, :],
                                                  start=True, stop=True))
        p.step("scalar", lambda: nc.scalar.mul(ssb[:, :], ps_1[:, :], 1.0 / (D * FF)))
        p.step("vector", lambda: nc.vector.tensor_scalar_max(ssb[:, :], ssb[:, :], 1e-8))
        p.step("vector", lambda: nc.vector.reciprocal(rs[:, :], ssb[:, :]))
        p.step("tensor", lambda: nc.tensor.matmul(ps_b[:, :], ones[0:1, 0:128], rs[:, :],
                                                  start=True, stop=True))
        p.step("scalar", lambda: nc.scalar.copy(rs128[:, :], ps_b[:, :]))
        for c in range(nchunk):
            p.dma(stage[:, 0:width], dram[c * 128:(c + 1) * 128, :])
            p.step("vector", lambda w=width: nc.vector.tensor_scalar(
                stage[:, 0:w], stage[:, 0:w], rs128[:, 0:1], None, ALU.mult))
            p.step("scalar", lambda w=width: nc.scalar.activation(
                tmpa[:, 0:w], stage[:, 0:w], AF.Abs))
            p.step("vector", lambda w=width: nc.vector.tensor_single_scalar(
                tmpa[:, 0:w], tmpa[:, 0:w], 0.5, ALU.is_ge))
            p.step("scalar", lambda w=width: nc.scalar.activation(
                stage[:, 0:w], stage[:, 0:w], AF.Sign))
            p.step("vector", lambda c=c, w=width: nc.vector.tensor_mul(
                dst[c][:, 0:w], tmpa[:, 0:w], stage[:, 0:w]))

    ternarize(w1t_d, 8, FF, w1s, "w1")
    ternarize(w2t_d, 32, D, w2s, "w2")

    for t in range(16):
        for dc in range(8):
            p.dma(xs[dc][:, :], xnO[dc * 128:(dc + 1) * 128, t * 128:(t + 1) * 128])
        for fc in range(32):
            def f(fc=fc):
                for dc in range(8):
                    nc.tensor.matmul(ps_h[:, :], w1s[dc][:, fc * 128:(fc + 1) * 128],
                                     xs[dc][:, :], start=(dc == 0), stop=False)
                return nc.tensor.matmul(ps_h[:, :], b1s[0:1, fc * 128:(fc + 1) * 128],
                                        ones[0:1, 0:128], start=False, stop=True)
            p.step("tensor", f)
            p.step("scalar", lambda fc=fc: nc.scalar.activation(hs[fc][:, :], ps_h[:, :], AF.Gelu))
        for nh in range(2):
            def f(nh=nh):
                for fc in range(32):
                    nc.tensor.matmul(ps_o[:, nh * 512:(nh + 1) * 512], hs[fc][:, :],
                                     w2s[fc][:, nh * 512:(nh + 1) * 512],
                                     start=(fc == 0), stop=False)
                return nc.tensor.matmul(ps_o[:, nh * 512:(nh + 1) * 512], ones[0:1, 0:128],
                                        b2s[0:1, nh * 512:(nh + 1) * 512], start=False, stop=True)
            p.step("tensor", f)
        p.step("scalar", lambda: nc.scalar.copy(osb[:, :], ps_o[:, :]))
        p.dma(f_out[t * 128:(t + 1) * 128, :], osb[:, :])
    nc = p.build(st)
    st.close()
    return nc


def build_comb0():
    p = Prog()
    nc = p.nc
    st = contextlib.ExitStack()
    x_d = nc.dram_tensor("x", [2048, D], F32, kind="ExternalInput").ap()
    sa_d = nc.dram_tensor("sa", [2048, D], F32, kind="ExternalInput").ap()
    sb_d = nc.dram_tensor("sb", [2048, D], F32, kind="ExternalInput").ap()
    f_d = nc.dram_tensor("f", [2048, D], F32, kind="ExternalInput").ap()
    x1_d = nc.dram_tensor("x1", [2048, D], F32, kind="ExternalOutput").ap()
    xn_d = nc.dram_tensor("xn", [2048, D], BF16, kind="ExternalOutput").ap()
    xt = st.enter_context(nc.sbuf_tensor("xt", [128, D], F32))
    sat = st.enter_context(nc.sbuf_tensor("sat", [128, D], F32))
    sbt = st.enter_context(nc.sbuf_tensor("sbt", [128, D], F32))
    ft = st.enter_context(nc.sbuf_tensor("ft", [128, D], F32))
    gt = st.enter_context(nc.sbuf_tensor("gt", [128, D], F32))
    xc = st.enter_context(nc.sbuf_tensor("xc", [128, D], F32))
    yt = st.enter_context(nc.sbuf_tensor("yt", [128, D], BF16))
    m = st.enter_context(nc.sbuf_tensor("m", [128, 1], F32))
    v = st.enter_context(nc.sbuf_tensor("v", [128, 1], F32))
    r = st.enter_context(nc.sbuf_tensor("r", [128, 1], F32))
    for t in range(16):
        sl = slice(t * 128, (t + 1) * 128)
        p.dma(xt[:, :], x_d[sl, :])
        p.dma(sat[:, :], sa_d[sl, :])
        p.dma(sbt[:, :], sb_d[sl, :])
        p.dma(ft[:, :], f_d[sl, :])
        p.step("vector", lambda: nc.vector.tensor_add(sat[:, :], sat[:, :], sbt[:, :]))
        p.step("scalar", lambda: nc.scalar.activation(gt[:, :], ft[:, :], AF.Silu))
        p.step("vector", lambda: nc.vector.tensor_mul(gt[:, :], gt[:, :], sat[:, :]))
        p.step("vector", lambda: nc.vector.tensor_add(xt[:, :], xt[:, :], gt[:, :]))
        p.dma(x1_d[sl, :], xt[:, :])
        p.step("vector", lambda: nc.vector.reduce_sum(m[:, :], xt[:, :], axis=AX))
        p.step("scalar", lambda: nc.scalar.mul(m[:, :], m[:, :], 1.0 / D))
        p.step("vector", lambda: nc.vector.tensor_scalar(xc[:, :], xt[:, :], m[:, 0:1], None, ALU.subtract))
        p.step("vector", lambda: nc.vector.tensor_mul(gt[:, :], xc[:, :], xc[:, :]))
        p.step("vector", lambda: nc.vector.reduce_sum(v[:, :], gt[:, :], axis=AX))
        p.step("scalar", lambda: nc.scalar.mul(v[:, :], v[:, :], 1.0 / D))
        p.step("vector", lambda: nc.vector.tensor_scalar_add(v[:, :], v[:, :], 1e-5))
        p.step("scalar", lambda: nc.scalar.activation(v[:, :], v[:, :], AF.Sqrt))
        p.step("vector", lambda: nc.vector.reciprocal(r[:, :], v[:, :]))
        p.step("vector", lambda: nc.vector.tensor_scalar(yt[:, :], xc[:, :], r[:, 0:1], None, ALU.mult))
        p.dma(xn_d[sl, :], yt[:, :])
    nc = p.build(st)
    st.close()
    return nc


def build_attn1():
    p = Prog()
    nc = p.nc
    st = contextlib.ExitStack()
    xnT = nc.dram_tensor("xnT", [D, S], BF16, kind="ExternalInput").ap()
    xnO = nc.dram_tensor("xnO", [D, 2048], BF16, kind="ExternalInput").ap()
    wq_d = nc.dram_tensor("wqt", [D, 512], F32, kind="ExternalInput").ap()
    wk_d = nc.dram_tensor("wkt", [D, 512], F32, kind="ExternalInput").ap()
    wv_d = nc.dram_tensor("wvt", [D, 512], F32, kind="ExternalInput").ap()
    wo_d = nc.dram_tensor("wot", [512, D], F32, kind="ExternalInput").ap()
    rb_d = nc.dram_tensor("rb", [32, 8], F32, kind="ExternalInput").ap()
    ot_d = nc.dram_tensor("ot", [32, 8448], F32, kind="ExternalInput").ap()
    idb_d = nc.dram_tensor("identb", [128, 128], BF16, kind="ExternalInput").ap()
    s_out = nc.dram_tensor("s", [2048, D], F32, kind="ExternalOutput").ap()
    brel = nc.dram_tensor("brel", [8, 8448], F32, kind="Internal").ap()

    stage = st.enter_context(nc.sbuf_tensor("stage", [128, 1024], F32))
    wqs = [st.enter_context(nc.sbuf_tensor(f"wqs{i}", [128, 512], BF16)) for i in range(8)]
    wks = [st.enter_context(nc.sbuf_tensor(f"wks{i}", [128, 512], BF16)) for i in range(8)]
    wvs = [st.enter_context(nc.sbuf_tensor(f"wvs{i}", [128, 512], BF16)) for i in range(8)]
    wos = [st.enter_context(nc.sbuf_tensor(f"wos{i}", [64, 1024], BF16)) for i in range(8)]
    kt = [st.enter_context(nc.sbuf_tensor(f"kt{i}", [128, S], BF16)) for i in range(4)]
    vsb = st.enter_context(nc.sbuf_tensor("vsb", [128, 32 * 512], BF16))
    xs = [st.enter_context(nc.sbuf_tensor(f"xs{i}", [128, 512], BF16)) for i in range(8)]
    qt = [st.enter_context(nc.sbuf_tensor(f"qt{i}", [128, 128], BF16)) for i in range(4)]
    sc = st.enter_context(nc.sbuf_tensor("sc", [128, S], F32))
    pb = st.enter_context(nc.sbuf_tensor("pb", [128, S], BF16))
    strip = st.enter_context(nc.sbuf_tensor("strip", [128, 8448], F32))
    rbs = st.enter_context(nc.sbuf_tensor("rbs", [32, 8], F32))
    ots = st.enter_context(nc.sbuf_tensor("ots", [32, 512], F32))
    brl = st.enter_context(nc.sbuf_tensor("brl", [8, 512], F32))
    idb = st.enter_context(nc.sbuf_tensor("idb", [128, 128], BF16))
    pt = st.enter_context(nc.sbuf_tensor("pt", [128, 128], BF16))
    yt = st.enter_context(nc.sbuf_tensor("yt", [64, 128], BF16))
    osb = st.enter_context(nc.sbuf_tensor("osb", [128, 1024], F32))
    mx = st.enter_context(nc.sbuf_tensor("mx", [128, 1], F32))
    nm = st.enter_context(nc.sbuf_tensor("nm", [128, 1], F32))
    sm = st.enter_context(nc.sbuf_tensor("sm", [128, 1], F32))
    rc = st.enter_context(nc.sbuf_tensor("rc", [128, 1], F32))
    tmp = st.enter_context(nc.sbuf_tensor("tmp", [128, 1024], F32))

    ps_sc = st.enter_context(nc.psum_tensor("ps_sc", [128, 512], F32))
    ps_tp = st.enter_context(nc.psum_tensor("ps_tp", [128, 128], BF16))
    ps_pv = st.enter_context(nc.psum_tensor("ps_pv", [64, 128], F32))
    ps_o = st.enter_context(nc.psum_tensor("ps_o", [128, 1024], F32))
    ps_p = st.enter_context(nc.psum_tensor("ps_p", [128, 512], F32))
    ps_b = st.enter_context(nc.psum_tensor("ps_b", [8, 512], F32))

    def ldcast(dst_bf, dram_ap, w):
        p.dma(stage[:, 0:w], dram_ap)
        p.step("vector", lambda d=dst_bf, w=w: nc.vector.tensor_copy(d[:, 0:w], stage[:, 0:w]))

    for i in range(8):
        ldcast(wqs[i], wq_d[i * 128:(i + 1) * 128, :], 512)
    for i in range(8):
        ldcast(wks[i], wk_d[i * 128:(i + 1) * 128, :], 512)
    for i in range(8):
        ldcast(wvs[i], wv_d[i * 128:(i + 1) * 128, :], 512)
    for i in range(8):
        p.dma(stage[0:64, 0:1024], wo_d[i * 64:(i + 1) * 64, :])
        p.step("vector", lambda i=i: nc.vector.tensor_copy(wos[i][:, :], stage[0:64, 0:1024]))
    p.dma(idb[:, :], idb_d[:, :])
    p.dma(rbs[:, :], rb_d[:, :])

    for c in range(17):
        w = 512 if c < 16 else 256
        p.dma(ots[:, 0:w], ot_d[:, c * 512:c * 512 + w])
        p.step("tensor", lambda w=w: nc.tensor.matmul(ps_b[:, 0:w], rbs[:, :], ots[:, 0:w],
                                                      start=True, stop=True))
        p.step("scalar", lambda w=w: nc.scalar.copy(brl[:, 0:w], ps_b[:, 0:w]))
        p.dma(brel[:, c * 512:c * 512 + w], brl[:, 0:w])

    for ncx in range(8):
        for dc in range(8):
            p.dma(xs[dc][:, :], xnT[dc * 128:(dc + 1) * 128, ncx * 512:(ncx + 1) * 512])
        for m4 in range(4):
            def f(m4=m4, ncx=ncx):
                for dc in range(8):
                    ins = nc.tensor.matmul(ps_p[:, :], wks[dc][:, m4 * 128:(m4 + 1) * 128],
                                           xs[dc][:, :], start=(dc == 0), stop=(dc == 7))
                return ins
            p.step("tensor", f)
            p.step("scalar", lambda m4=m4, ncx=ncx: nc.scalar.copy(
                kt[m4][:, ncx * 512:(ncx + 1) * 512], ps_p[:, :]))
        for nt in range(4):
            ng = ncx * 4 + nt
            def f(nt=nt):
                for dc in range(8):
                    ins = nc.tensor.matmul(ps_p[:, :], xs[dc][:, nt * 128:(nt + 1) * 128],
                                           wvs[dc][:, :], start=(dc == 0), stop=(dc == 7))
                return ins
            p.step("tensor", f)
            p.step("scalar", lambda ng=ng: nc.scalar.copy(
                vsb[:, ng * 512:(ng + 1) * 512], ps_p[:, :]))

    for h in range(8):
        for prow in range(128):
            p.dma(strip[prow:prow + 1, 255:8191],
                  brel[h:h + 1, 383 - prow:383 - prow + 7936])
        for j in range(16):
            off = 4095 - 256 * j
            for dc in range(8):
                p.dma(xs[dc][:, 0:128], xnO[dc * 128:(dc + 1) * 128, j * 128:(j + 1) * 128])
            for m4 in range(4):
                def f(m4=m4):
                    for dc in range(8):
                        ins = nc.tensor.matmul(ps_p[:, 0:128], wqs[dc][:, m4 * 128:(m4 + 1) * 128],
                                               xs[dc][:, 0:128], start=(dc == 0), stop=(dc == 7))
                    return ins
                p.step("tensor", f)
                p.step("scalar", lambda m4=m4: nc.scalar.copy(qt[m4][:, :], ps_p[:, 0:128]))
            hh = h * 64
            qc, qo = hh // 128, hh % 128
            for c in range(8):
                def f(c=c, qc=qc, qo=qo):
                    return nc.tensor.matmul(ps_sc[:, :], qt[qc][qo:qo + 64, :],
                                            kt[qc][qo:qo + 64, c * 512:(c + 1) * 512],
                                            start=True, stop=True, skip_group_check=True)
                p.step("tensor", f)
                p.step("vector", lambda c=c, off=off: nc.vector.scalar_tensor_tensor(
                    sc[:, c * 512:(c + 1) * 512], ps_sc[:, :], 1.0,
                    strip[:, off + c * 512: off + (c + 1) * 512], ALU.mult, ALU.add))
            p.step("vector", lambda: nc.vector.reduce_max(mx[:, :], sc[:, :], axis=AX))
            p.step("scalar", lambda: nc.scalar.mul(nm[:, :], mx[:, :], -1.0))
            p.step("scalar", lambda: nc.scalar.activation(pb[:, :], sc[:, :], AF.Exp, bias=nm[:, 0:1]))
            p.step("vector", lambda: nc.vector.reduce_sum(sm[:, :], pb[:, :], axis=AX))
            p.step("vector", lambda: nc.vector.reciprocal(rc[:, :], sm[:, :]))
            p.step("vector", lambda: nc.vector.tensor_scalar(pb[:, :], pb[:, :], rc[:, 0:1], None, ALU.mult))
            for kc in range(32):
                p.step("tensor", lambda kc=kc: nc.tensor.transpose(
                    ps_tp[:, :], pb[:, kc * 128:(kc + 1) * 128], idb[:, :]))
                p.step("scalar", lambda: nc.scalar.copy(pt[:, :], ps_tp[:, :]))
                p.step("tensor", lambda kc=kc, hh=hh: nc.tensor.matmul(
                    ps_pv[:, :], vsb[:, kc * 512 + hh:kc * 512 + hh + 64], pt[:, :],
                    start=(kc == 0), stop=(kc == 31), skip_group_check=True))
            p.step("scalar", lambda: nc.scalar.copy(yt[:, :], ps_pv[:, :]))
            for nh in range(2):
                p.step("tensor", lambda nh=nh, h=h: nc.tensor.matmul(
                    ps_o[:, nh * 512:(nh + 1) * 512], yt[:, :],
                    wos[h][:, nh * 512:(nh + 1) * 512],
                    start=True, stop=True, skip_group_check=True))
            if h == 0:
                p.step("scalar", lambda: nc.scalar.copy(osb[:, :], ps_o[:, :]))
            else:
                p.dma(tmp[:, :], s_out[j * 128:(j + 1) * 128, :])
                p.step("vector", lambda: nc.vector.tensor_add(tmp[:, :], tmp[:, :], ps_o[:, :]))
                p.step("scalar", lambda: nc.scalar.copy(osb[:, :], tmp[:, :]))
            p.dma(s_out[j * 128:(j + 1) * 128, :], osb[:, :])
    nc = p.build(st)
    st.close()
    return nc


def build_moe():
    p = Prog()
    nc = p.nc
    st = contextlib.ExitStack()
    xnP = nc.dram_tensor("xnP", [D, S], BF16, kind="ExternalInput").ap()
    gq_d = nc.dram_tensor("gqt", [D, 16], F32, kind="ExternalInput").ap()
    lt_d = nc.dram_tensor("ltemp", [1, 1], F32, kind="ExternalInput").ap()
    w1_d = nc.dram_tensor("w1", [4, D, FF], F32, kind="ExternalInput").ap()
    b1_d = nc.dram_tensor("b1", [4, FF], F32, kind="ExternalInput").ap()
    w2_d = nc.dram_tensor("w2", [4, FF, D], F32, kind="ExternalInput").ap()
    b2_d = nc.dram_tensor("b2", [4, D], F32, kind="ExternalInput").ap()
    idf_d = nc.dram_tensor("identf", [128, 128], F32, kind="ExternalInput").ap()
    m_out = nc.dram_tensor("m", [1024, D], F32, kind="ExternalOutput").ap()

    stage = st.enter_context(nc.sbuf_tensor("stage", [128, 1024], F32))
    w1s = [st.enter_context(nc.sbuf_tensor(f"w1s{i}", [128, FF], BF16)) for i in range(8)]
    w2s = [st.enter_context(nc.sbuf_tensor(f"w2s{i}", [128, D], BF16)) for i in range(32)]
    hs = [st.enter_context(nc.sbuf_tensor(f"hs{i}", [128, 256], BF16)) for i in range(16)]
    xs = [st.enter_context(nc.sbuf_tensor(f"xs{i}", [128, 256], BF16)) for i in range(8)]
    oacc = st.enter_context(nc.sbuf_tensor("oacc", [128, 8 * D], F32))
    gqs = [st.enter_context(nc.sbuf_tensor(f"gqs{i}", [128, 16], BF16)) for i in range(8)]
    at = st.enter_context(nc.sbuf_tensor("at", [128, 512], F32))
    a16 = st.enter_context(nc.sbuf_tensor("a16", [16, S], F32))
    tt = st.enter_context(nc.sbuf_tensor("tt", [128, 16], F32))
    tt2 = st.enter_context(nc.sbuf_tensor("tt2", [128, 16], F32))
    gts = st.enter_context(nc.sbuf_tensor("gts", [128, 8 * 16], F32))
    idf = st.enter_context(nc.sbuf_tensor("idf", [128, 128], F32))
    lts = st.enter_context(nc.sbuf_tensor("lts", [1, 1], F32))
    it128 = st.enter_context(nc.sbuf_tensor("it128", [128, 1], F32))
    ones = st.enter_context(nc.sbuf_tensor("ones", [1, 512], F32))
    mxp = st.enter_context(nc.sbuf_tensor("mxp", [128, 1], F32))
    smp = st.enter_context(nc.sbuf_tensor("smp", [128, 1], F32))
    mx16 = st.enter_context(nc.sbuf_tensor("mx16", [16, 1], F32))
    sm16 = st.enter_context(nc.sbuf_tensor("sm16", [16, 1], F32))
    lo = st.enter_context(nc.sbuf_tensor("lo", [16, 1], F32))
    hi = st.enter_context(nc.sbuf_tensor("hi", [16, 1], F32))
    mid = st.enter_context(nc.sbuf_tensor("mid", [16, 1], F32))
    cnt = st.enter_context(nc.sbuf_tensor("cnt", [16, 1], F32))
    cnd = st.enter_context(nc.sbuf_tensor("cnd", [16, 1], F32))
    dd1 = st.enter_context(nc.sbuf_tensor("dd1", [16, 1], F32))
    dd2 = st.enter_context(nc.sbuf_tensor("dd2", [16, 1], F32))
    ps_at = st.enter_context(nc.psum_tensor("ps_at", [128, 16], F32))
    ps_t1 = st.enter_context(nc.psum_tensor("ps_t1", [16, 128], F32))
    ps_t2 = st.enter_context(nc.psum_tensor("ps_t2", [128, 16], F32))
    ps_h = st.enter_context(nc.psum_tensor("ps_h", [128, 256], F32))
    ps_o0 = st.enter_context(nc.psum_tensor("ps_o0", [128, 1024], F32))
    ps_o1 = st.enter_context(nc.psum_tensor("ps_o1", [128, 1024], F32))

    p.step("vector", lambda: nc.vector.memset(ones[:, :], 1.0))
    p.dma(idf[:, :], idf_d[:, :])
    p.dma(lts[:, :], lt_d[:, :])
    p.step("scalar", lambda: nc.scalar.activation(lts[:, :], lts[:, :], AF.Exp, scale=-1.0))
    p.step("tensor", lambda: nc.tensor.matmul(ps_at[:, 0:1], ones[0:1, 0:128], lts[:, :],
                                              start=True, stop=True))
    p.step("scalar", lambda: nc.scalar.copy(it128[:, :], ps_at[:, 0:1]))
    for dc in range(8):
        p.dma(stage[:, 0:16], gq_d[dc * 128:(dc + 1) * 128, :])
        p.step("vector", lambda dc=dc: nc.vector.tensor_copy(gqs[dc][:, :], stage[:, 0:16]))

    for nt in range(32):
        for dc in range(8):
            p.dma(xs[dc][:, 0:128], xnP[dc * 128:(dc + 1) * 128, nt * 128:(nt + 1) * 128])
        def f():
            for dc in range(8):
                ins = nc.tensor.matmul(ps_at[:, :], xs[dc][:, 0:128], gqs[dc][:, :],
                                       start=(dc == 0), stop=(dc == 7))
            return ins
        p.step("tensor", f)
        p.step("vector", lambda nt=nt: nc.vector.tensor_scalar(
            at[:, nt * 16:(nt + 1) * 16], ps_at[:, :], it128[:, 0:1], None, ALU.mult))

    for it in range(6):
        for nt in range(32):
            asl = at[:, nt * 16:(nt + 1) * 16]
            p.step("vector", lambda asl=asl: nc.vector.reduce_max(mxp[:, :], asl, axis=AX))
            p.step("vector", lambda asl=asl: nc.vector.tensor_scalar(
                tt[:, :], asl, mxp[:, 0:1], None, ALU.subtract))
            p.step("scalar", lambda: nc.scalar.activation(tt2[:, :], tt[:, :], AF.Exp))
            p.step("vector", lambda: nc.vector.reduce_sum(smp[:, :], tt2[:, :], axis=AX))
            p.step("scalar", lambda: nc.scalar.activation(smp[:, :], smp[:, :], AF.Ln))
            p.step("vector", lambda asl=asl: nc.vector.tensor_scalar(
                asl, tt[:, :], smp[:, 0:1], None, ALU.subtract))
            p.step("tensor", lambda asl=asl: nc.tensor.transpose(ps_t1[:, :], asl, idf[:, :]))
            p.step("scalar", lambda nt=nt: nc.scalar.copy(a16[:, nt * 128:(nt + 1) * 128], ps_t1[:, :]))
        p.step("vector", lambda: nc.vector.reduce_max(mx16[:, :], a16[:, :], axis=AX))
        p.step("vector", lambda: nc.vector.tensor_scalar(
            a16[:, :], a16[:, :], mx16[:, 0:1], None, ALU.subtract))
        p.step("scalar", lambda: nc.scalar.activation(oacc[0:16, 0:4096], a16[:, :], AF.Exp))
        p.step("vector", lambda: nc.vector.reduce_sum(sm16[:, :], oacc[0:16, 0:4096], axis=AX))
        p.step("scalar", lambda: nc.scalar.activation(sm16[:, :], sm16[:, :], AF.Ln))
        p.step("vector", lambda: nc.vector.tensor_scalar(
            a16[:, :], a16[:, :], sm16[:, 0:1], None, ALU.subtract))
        if it < 5:
            for nt in range(32):
                p.step("tensor", lambda nt=nt: nc.tensor.transpose(
                    ps_t2[:, :], a16[:, nt * 128:(nt + 1) * 128], idf[0:16, 0:16]))
                p.step("vector", lambda nt=nt: nc.vector.tensor_copy(
                    at[:, nt * 16:(nt + 1) * 16], ps_t2[:, :]))

    p.step("scalar", lambda: nc.scalar.activation(a16[:, :], a16[:, :], AF.Exp))
    p.step("vector", lambda: nc.vector.memset(lo[:, :], 0.0))
    p.step("vector", lambda: nc.vector.reduce_max(hi[:, :], a16[:, :], axis=AX))
    for _ in range(30):
        p.step("vector", lambda: nc.vector.tensor_add(mid[:, :], lo[:, :], hi[:, :]))
        p.step("scalar", lambda: nc.scalar.mul(mid[:, :], mid[:, :], 0.5))
        p.step("vector", lambda: nc.vector.tensor_single_scalar(
            oacc[0:16, 0:4096], a16[:, :], mid[:, 0:1], ALU.is_gt))
        p.step("vector", lambda: nc.vector.reduce_sum(cnt[:, :], oacc[0:16, 0:4096], axis=AX))
        p.step("vector", lambda: nc.vector.tensor_single_scalar(
            cnd[:, :], cnt[:, :], float(CAP), ALU.is_gt))
        p.step("vector", lambda: nc.vector.tensor_sub(dd1[:, :], mid[:, :], lo[:, :]))
        p.step("vector", lambda: nc.vector.tensor_sub(dd2[:, :], hi[:, :], mid[:, :]))
        p.step("vector", lambda: nc.vector.scalar_tensor_tensor(
            lo[:, :], dd1[:, :], cnd[:, 0:1], lo[:, :], ALU.mult, ALU.add))
        p.step("vector", lambda: nc.vector.scalar_tensor_tensor(
            hi[:, :], dd2[:, :], cnd[:, 0:1], mid[:, :], ALU.mult, ALU.add))
    p.step("vector", lambda: nc.vector.tensor_single_scalar(
        oacc[0:16, 0:4096], a16[:, :], hi[:, 0:1], ALU.is_gt))
    p.step("vector", lambda: nc.vector.tensor_mul(oacc[0:16, 4096:8192], oacc[0:16, 0:4096], a16[:, :]))
    for ntl in range(8):
        p.step("tensor", lambda ntl=ntl: nc.tensor.transpose(
            ps_t2[:, :], oacc[0:16, 4096 + ntl * 128:4096 + (ntl + 1) * 128], idf[0:16, 0:16]))
        p.step("scalar", lambda ntl=ntl: nc.scalar.copy(gts[:, ntl * 16:(ntl + 1) * 16], ps_t2[:, :]))

    for e in range(4):
        for dc in range(8):
            for q4 in range(4):
                p.dma(stage[:, :], w1_d[e, dc * 128:(dc + 1) * 128, q4 * 1024:(q4 + 1) * 1024])
                p.step("vector", lambda dc=dc, q4=q4: nc.vector.tensor_copy(
                    w1s[dc][:, q4 * 1024:(q4 + 1) * 1024], stage[:, :]))
        for fc in range(32):
            p.dma(stage[:, :], w2_d[e, fc * 128:(fc + 1) * 128, :])
            p.step("vector", lambda fc=fc: nc.vector.tensor_copy(w2s[fc][:, :], stage[:, :]))
        p.dma(a16[0:1, 0:FF], b1_d[e:e + 1, :])
        p.dma(stage[0:1, 0:D], b2_d[e:e + 1, :])
        for tg in range(4):
            for dc in range(8):
                p.dma(xs[dc][:, :], xnP[dc * 128:(dc + 1) * 128, tg * 256:(tg + 1) * 256])
            for half in range(2):
                for fl in range(16):
                    fc = half * 16 + fl
                    def f(fc=fc):
                        for dc in range(8):
                            nc.tensor.matmul(ps_h[:, :], w1s[dc][:, fc * 128:(fc + 1) * 128],
                                             xs[dc][:, :], start=(dc == 0), stop=False)
                        return nc.tensor.matmul(ps_h[:, :], a16[0:1, fc * 128:(fc + 1) * 128],
                                                ones[0:1, 0:256], start=False, stop=True)
                    p.step("tensor", f)
                    p.step("scalar", lambda fl=fl: nc.scalar.activation(
                        hs[fl][:, :], ps_h[:, :], AF.Gelu_apprx_tanh))
                for t2 in range(2):
                    pso = ps_o0 if t2 == 0 else ps_o1
                    for nh in range(2):
                        if half == 0:
                            def f0(t2=t2, nh=nh, pso=pso):
                                for fl in range(16):
                                    ins = nc.tensor.matmul(pso[:, nh * 512:(nh + 1) * 512],
                                                           hs[fl][:, t2 * 128:(t2 + 1) * 128],
                                                           w2s[fl][:, nh * 512:(nh + 1) * 512],
                                                           start=(fl == 0), stop=False,
                                                           skip_group_check=True)
                                return ins
                            p.step("tensor", f0)
                        else:
                            def f1(t2=t2, nh=nh, pso=pso):
                                for fl in range(16):
                                    nc.tensor.matmul(pso[:, nh * 512:(nh + 1) * 512],
                                                     hs[fl][:, t2 * 128:(t2 + 1) * 128],
                                                     w2s[16 + fl][:, nh * 512:(nh + 1) * 512],
                                                     start=False, stop=False,
                                                     skip_group_check=True)
                                return nc.tensor.matmul(pso[:, nh * 512:(nh + 1) * 512],
                                                        ones[0:1, 0:128],
                                                        stage[0:1, nh * 512:(nh + 1) * 512],
                                                        start=False, stop=True,
                                                        skip_group_check=True)
                            p.step("tensor", f1)
            for t2 in range(2):
                t = tg * 2 + t2
                pso = ps_o0 if t2 == 0 else ps_o1
                if e == 0:
                    p.step("vector", lambda t=t, e=e, pso=pso: nc.vector.tensor_scalar(
                        oacc[:, t * D:(t + 1) * D], pso[:, :],
                        gts[:, t * 16 + e:t * 16 + e + 1], None, ALU.mult))
                else:
                    p.step("vector", lambda t=t, e=e, pso=pso: nc.vector.scalar_tensor_tensor(
                        oacc[:, t * D:(t + 1) * D], pso[:, :],
                        gts[:, t * 16 + e:t * 16 + e + 1],
                        oacc[:, t * D:(t + 1) * D], ALU.mult, ALU.add))
    for t in range(8):
        p.dma(m_out[t * 128:(t + 1) * 128, :], oacc[:, t * D:(t + 1) * D])
    nc = p.build(st)
    st.close()
    return nc


def build_comb1():
    p = Prog()
    nc = p.nc
    st = contextlib.ExitStack()
    x_d = nc.dram_tensor("x1", [2048, D], F32, kind="ExternalInput").ap()
    sa_d = nc.dram_tensor("sa", [2048, D], F32, kind="ExternalInput").ap()
    sb_d = nc.dram_tensor("sb", [2048, D], F32, kind="ExternalInput").ap()
    mds = [nc.dram_tensor(f"m{i}", [2048, D], F32, kind="ExternalInput").ap() for i in range(4)]
    o_d = nc.dram_tensor("o", [2048, D], F32, kind="ExternalOutput").ap()
    xt = st.enter_context(nc.sbuf_tensor("xt", [128, D], F32))
    sat = st.enter_context(nc.sbuf_tensor("sat", [128, D], F32))
    sbt = st.enter_context(nc.sbuf_tensor("sbt", [128, D], F32))
    mt = st.enter_context(nc.sbuf_tensor("mt", [128, D], F32))
    m2t = st.enter_context(nc.sbuf_tensor("m2t", [128, D], F32))
    gt = st.enter_context(nc.sbuf_tensor("gt", [128, D], F32))
    for t in range(16):
        sl = slice(t * 128, (t + 1) * 128)
        p.dma(xt[:, :], x_d[sl, :])
        p.dma(sat[:, :], sa_d[sl, :])
        p.dma(sbt[:, :], sb_d[sl, :])
        p.dma(mt[:, :], mds[0][sl, :])
        for i in range(1, 4):
            p.dma(m2t[:, :], mds[i][sl, :])
            p.step("vector", lambda: nc.vector.tensor_add(mt[:, :], mt[:, :], m2t[:, :]))
        p.step("vector", lambda: nc.vector.tensor_add(sat[:, :], sat[:, :], sbt[:, :]))
        p.step("scalar", lambda: nc.scalar.activation(gt[:, :], mt[:, :], AF.Silu))
        p.step("vector", lambda: nc.vector.tensor_mul(gt[:, :], gt[:, :], sat[:, :]))
        p.step("vector", lambda: nc.vector.tensor_add(xt[:, :], xt[:, :], gt[:, :]))
        p.dma(o_d[sl, :], xt[:, :])
    nc = p.build(st)
    st.close()
    return nc


def _bucket_fn(rp):
    nb = 16
    rb = (rp > 0).astype(np.int64) * nb
    n = np.abs(rp)
    max_exact = nb // 2
    is_small = n < max_exact
    with np.errstate(divide="ignore"):
        large = max_exact + (np.log(np.maximum(n, 1).astype(np.float64) / max_exact)
                             / math.log(128 / max_exact) * (nb - max_exact)).astype(np.int64)
    large = np.minimum(large, nb - 1)
    return rb + np.where(is_small, n, large)


_CACHE = {}


def _get(name, fn):
    if name not in _CACHE:
        _CACHE[name] = fn()
    return _CACHE[name]


def _run(nc, in_maps, names):
    res = run_bass_kernel_spmd(nc, in_maps, core_ids=list(range(8)))
    return [[np.asarray(r[n]) for n in names] for r in res.results]


def kernel(L0_attn_x_bld, L0_attn_Wq, L0_attn_Wkv_down, L0_attn_Wk_up, L0_attn_Wv_up,
           L0_attn_Wout, L0_ffn_W1, L0_ffn_b1, L0_ffn_W2, L0_ffn_b2,
           L1_attn_Wq, L1_attn_Wk, L1_attn_Wv, L1_attn_Wo, L1_attn_rel_bias,
           L1_ffn_gate_query, L1_ffn_log_temp, L1_ffn_w1, L1_ffn_b1, L1_ffn_w2, L1_ffn_b2):
    f32 = lambda a: np.ascontiguousarray(np.asarray(a, dtype=np.float32))
    x = f32(L0_attn_x_bld)
    Wq0 = f32(L0_attn_Wq).reshape(D, H * HD)
    Wkv = f32(L0_attn_Wkv_down)
    Wku = f32(L0_attn_Wk_up).reshape(LD, H * HD)
    Wvu = f32(L0_attn_Wv_up).reshape(LD, H * HD)
    Wo0 = f32(L0_attn_Wout).reshape(H * HD, D)
    W1T = np.ascontiguousarray(f32(L0_ffn_W1).T)
    W2T = np.ascontiguousarray(f32(L0_ffn_W2).T)
    b1_0 = f32(L0_ffn_b1).reshape(1, FF)
    b2_0 = f32(L0_ffn_b2).reshape(1, D)
    Wq1T = np.ascontiguousarray(f32(L1_attn_Wq).T)
    Wk1T = np.ascontiguousarray(f32(L1_attn_Wk).T)
    Wv1T = np.ascontiguousarray(f32(L1_attn_Wv).T)
    Wo1T = np.ascontiguousarray(f32(L1_attn_Wo).T)
    relb = f32(L1_attn_rel_bias)
    gqT = np.ascontiguousarray(f32(L1_ffn_gate_query).T)
    ltemp = f32(L1_ffn_log_temp).reshape(1, 1)
    w1m = f32(L1_ffn_w1)
    b1m = f32(L1_ffn_b1)
    w2m = f32(L1_ffn_w2)
    b2m = f32(L1_ffn_b2)

    identb = np.eye(128, dtype=BF)
    identf = np.eye(128, dtype=np.float32)
    iota = np.broadcast_to(np.arange(S, dtype=np.float32), (128, S)).copy()

    ots = {}
    for par in (0, 1):
        i_idx = np.arange(8448, dtype=np.int64)
        d_rel = i_idx - 128 * par - 4223
        valid = (d_rel >= -4095) & (d_rel <= 4095)
        bk = _bucket_fn(d_rel)
        o = np.zeros((32, 8448), dtype=np.float32)
        o[bk[valid], i_idx[valid]] = 1.0
        ots[par] = o

    def rows_of(b, par, arr):
        return np.ascontiguousarray(arr[b].reshape(32, 128, D)[par::2].reshape(2048, D))

    def qidx_of(par):
        q = np.empty((128, 16), dtype=np.float32)
        for j in range(16):
            q[:, j] = (2 * j + par) * 128 + np.arange(128)
        return q

    cores = [(c // 2, c % 2) for c in range(8)]

    nc_ln = _get("ln", build_ln)
    outs = _run(nc_ln, [{"x": rows_of(b, par, x)} for (b, par) in cores], ["y"])
    xn0 = np.empty((B, S, D), dtype=BF)
    for c, (b, par) in enumerate(cores):
        xn0[b].reshape(32, 128, D)[par::2] = outs[c][0].reshape(16, 128, D)
    xn0T = [np.ascontiguousarray(xn0[b].T) for b in range(B)]

    nc_a0 = _get("a0", build_attn0)
    s0 = []
    for g in range(2):
        hs = slice(g * 512, (g + 1) * 512)
        ims = []
        for (b, par) in cores:
            own = np.ascontiguousarray(
                xn0T[b].reshape(D, 32, 128)[:, par::2, :].reshape(D, 2048))
            ims.append({"xnT": xn0T[b], "xnO": own,
                        "wq": np.ascontiguousarray(Wq0[:, hs]),
                        "wkv": Wkv,
                        "wku": np.ascontiguousarray(Wku[:, hs]),
                        "wvu": np.ascontiguousarray(Wvu[:, hs]),
                        "wo": np.ascontiguousarray(Wo0[hs, :]),
                        "qidx": qidx_of(par), "iota": iota, "identb": identb})
        s0.append(_run(nc_a0, ims, ["s"]))

    nc_f0 = _get("f0", build_ffn0)
    ims = []
    for (b, par) in cores:
        own = np.ascontiguousarray(xn0T[b].reshape(D, 32, 128)[:, par::2, :].reshape(D, 2048))
        ims.append({"xnO": own, "w1t": W1T, "w2t": W2T, "b1": b1_0, "b2": b2_0})
    f0 = _run(nc_f0, ims, ["f"])

    nc_c0 = _get("c0", build_comb0)
    ims = []
    for c, (b, par) in enumerate(cores):
        ims.append({"x": rows_of(b, par, x), "sa": s0[0][c][0], "sb": s0[1][c][0],
                    "f": f0[c][0]})
    c0 = _run(nc_c0, ims, ["x1", "xn"])
    x1 = np.empty((B, S, D), dtype=np.float32)
    xn1 = np.empty((B, S, D), dtype=BF)
    for c, (b, par) in enumerate(cores):
        x1[b].reshape(32, 128, D)[par::2] = c0[c][0].reshape(16, 128, D)
        xn1[b].reshape(32, 128, D)[par::2] = c0[c][1].reshape(16, 128, D)
    xn1T = [np.ascontiguousarray(xn1[b].T) for b in range(B)]

    nc_a1 = _get("a1", build_attn1)
    s1 = []
    for g in range(2):
        hs = slice(g * 512, (g + 1) * 512)
        ims = []
        for (b, par) in cores:
            own = np.ascontiguousarray(
                xn1T[b].reshape(D, 32, 128)[:, par::2, :].reshape(D, 2048))
            ims.append({"xnT": xn1T[b], "xnO": own,
                        "wqt": np.ascontiguousarray(Wq1T[:, hs]),
                        "wkt": np.ascontiguousarray(Wk1T[:, hs]),
                        "wvt": np.ascontiguousarray(Wv1T[:, hs]),
                        "wot": np.ascontiguousarray(Wo1T[hs, :]),
                        "rb": np.ascontiguousarray(relb[:, g * 8:(g + 1) * 8]),
                        "ot": ots[par], "identb": identb})
        s1.append(_run(nc_a1, ims, ["s"]))

    nc_m = _get("moe", build_moe)
    moe_parts = [[None] * 4 for _ in range(8)]
    for r in range(2):
        for eq in range(4):
            es = list(range(eq * 4, (eq + 1) * 4)) + [e for e in range(16) if not (eq * 4 <= e < (eq + 1) * 4)]
            gq_perm = np.ascontiguousarray(gqT[:, es])
            ims = []
            for (b, par) in cores:
                ownh = np.arange(32)[par::2][r * 8:(r + 1) * 8]
                rest = np.array([t for t in range(32) if t not in set(ownh)])
                order = np.concatenate([ownh, rest])
                xp = np.ascontiguousarray(
                    xn1T[b].reshape(D, 32, 128)[:, order, :].reshape(D, S))
                ims.append({"xnP": xp, "gqt": gq_perm, "ltemp": ltemp,
                            "w1": np.ascontiguousarray(w1m[es[:4]]),
                            "b1": np.ascontiguousarray(b1m[es[:4]]),
                            "w2": np.ascontiguousarray(w2m[es[:4]]),
                            "b2": np.ascontiguousarray(b2m[es[:4]]),
                            "identf": identf})
            mo = _run(nc_m, ims, ["m"])
            for c in range(8):
                if moe_parts[c][eq] is None:
                    moe_parts[c][eq] = np.empty((2048, D), dtype=np.float32)
                moe_parts[c][eq][r * 1024:(r + 1) * 1024] = mo[c][0]

    nc_c1 = _get("c1", build_comb1)
    ims = []
    for c, (b, par) in enumerate(cores):
        ims.append({"x1": rows_of(b, par, x1), "sa": s1[0][c][0], "sb": s1[1][c][0],
                    "m0": moe_parts[c][0], "m1": moe_parts[c][1],
                    "m2": moe_parts[c][2], "m3": moe_parts[c][3]})
    fin = _run(nc_c1, ims, ["o"])
    out = np.empty((B, S, D), dtype=np.float32)
    for c, (b, par) in enumerate(cores):
        out[b].reshape(32, 128, D)[par::2] = fin[c][0].reshape(16, 128, D)
    return out
